# version 07
# baseline (speedup 1.0000x reference)
import contextlib
import numpy as np
import concourse.bass as bass
import concourse.mybir as mybir
from concourse.bass_utils import run_bass_kernel_spmd

F32 = mybir.dt.float32
BF16 = mybir.dt.bfloat16
AF = mybir.ActivationFunctionType
ALU = mybir.AluOpType
AX = mybir.AxisListType

P = 128
T = 2048
D = 1024
NT = 16
ND = 8
H = 8
E = 8
NQ = 4
FT_Q = 8
TC = 8
NIDX = E * NQ

WG_MQ, WG_MK, WG_MV, WG_RWT = 16, 32, 48, 64
def wg_q(idx):
    return 64 + 32 * idx + 32
N_PJEV = 64
DEBUG_STAGE = None


def build_nc():
    nc = bass.Bass("TRN2", target_bir_lowering=False, detect_race_conditions=False)

    x_d = nc.dram_tensor("x", [T, D], F32, kind="ExternalInput").ap()
    mq_d = nc.dram_tensor("mq", [D, 512], F32, kind="ExternalInput").ap()
    mk_d = nc.dram_tensor("mk", [D, 512], F32, kind="ExternalInput").ap()
    mv_d = nc.dram_tensor("mv", [D, D], F32, kind="ExternalInput").ap()
    rwt_d = nc.dram_tensor("rwt", [D, E], F32, kind="ExternalInput").ap()
    esc_d = nc.dram_tensor("esc", [P, E], F32, kind="ExternalInput").ap()
    w1t_d = nc.dram_tensor("w1t", [E, D, 4096], F32, kind="ExternalInput").ap()
    w2t_d = nc.dram_tensor("w2t", [E, 4096, D], F32, kind="ExternalInput").ap()
    out_d = nc.dram_tensor("out", [T, D], F32, kind="ExternalOutput").ap()
    if DEBUG_STAGE == "attn":
        dbg = {nm: nc.dram_tensor(nm, shp, dt, kind="ExternalOutput").ap()
               for nm, shp, dt in [("d_xnt", [P, ND * T], BF16), ("d_vp", [P, NT * D], BF16),
                                   ("d_qt", [P, 4 * T], BF16), ("d_kt", [P, 4 * T], BF16),
                                   ("d_comb", [P, NT * E], F32),
                                   ("d_s1", [P, NT], F32), ("d_nm", [P, NT], F32),
                                   ("d_ss", [P, NT], F32), ("d_veps", [P, NT], F32),
                                   ("d_sd", [P, NT], F32), ("d_rstd", [P, NT], F32),
                                   ("d_nmrs", [P, NT], F32)]}

    ctx = contextlib.ExitStack()
    with ctx:
        sb = lambda name, shape, dt: ctx.enter_context(nc.sbuf_tensor(name, shape, dt))
        X = sb("X", [P, NT * D], F32)
        XNT = sb("XNT", [P, ND * T], BF16)
        COMB = sb("COMB", [P, NT * E], F32)
        ESC = sb("ESC", [P, E], F32)
        RWTS = sb("RWTS", [P, ND * E], BF16)
        S1 = sb("S1", [P, NT], F32)
        NM = sb("NM", [P, NT], F32)
        SS = sb("SS", [P, NT], F32)
        VEPS = sb("VEPS", [P, NT], F32)
        SD = sb("SD", [P, NT], F32)
        RSTD = sb("RSTD", [P, NT], F32)
        NMRS = sb("NMRS", [P, NT], F32)
        MSQ = sb("MSQ", [P, NT], F32)
        V2 = sb("V2", [P, NT], F32)
        NM2 = sb("NM2", [P, 2], F32)
        NMmin = sb("NMmin", [P, 1], F32)
        NMmin8 = sb("NMmin8", [P, 1], F32)
        Zs = sb("Zs", [P, 1], F32)
        Rz = sb("Rz", [P, 1], F32)
        NMX = [sb(f"NMX{i}", [P, 1], F32) for i in range(2)]
        PBUF = [sb(f"PBUF{i}", [P, E], F32) for i in range(2)]
        SZ = [sb(f"SZ{i}", [P, 1], F32) for i in range(2)]
        RZr = [sb(f"RZr{i}", [P, 1], F32) for i in range(2)]
        PRB = [sb(f"PRB{i}", [P, E], F32) for i in range(2)]
        M12 = [sb(f"M12{i}", [P, 2], F32) for i in range(2)]
        MSK = [sb(f"MSK{i}", [P, E], F32) for i in range(2)]
        MKD = [sb(f"MKD{i}", [P, E], F32) for i in range(2)]
        EM2 = [sb(f"EM2{i}", [P, 2], F32) for i in range(2)]
        EPB = [sb(f"EPB{i}", [P, E], F32) for i in range(2)]
        DD = [sb(f"DD{i}", [P, 1], F32) for i in range(2)]
        RD = [sb(f"RD{i}", [P, 1], F32) for i in range(2)]
        CWA = [sb(f"CWA{i}", [P, E], F32) for i in range(2)]
        Z2p = [sb(f"Z2p{i}", [P, 2], F32) for i in range(2)]
        ZZ = sb("ZZ", [P, 512], BF16)
        CWB = [sb(f"CWB{i}", [P, E], F32) for i in range(2)]

        sem = {}
        for name in ["s_x", "s_hw", "s_wg", "s_xns", "s_xnt", "s_ln1", "s_ln2",
                     "s_ln3", "s_ln4", "s_pj", "s_pjev", "s_rt", "s_rv", "s_rs",
                     "s_gather", "s_schead", "s_mm", "s_vmax", "s_exp", "s_scl",
                     "s_av", "s_avev", "s_qdone", "s_m1", "s_silu", "s_m2",
                     "s_yev", "s_out", "s_zz"]:
            sem[name] = ctx.enter_context(nc.semaphore(name))

        blk = ctx.enter_context(nc.Block())

        with (nc.sbuf_tensor("QT", [P, 4 * T], BF16) as QT,
              nc.sbuf_tensor("KT", [P, 4 * T], BF16) as KT,
              nc.sbuf_tensor("VP", [P, NT * D], BF16) as VP):

            p1 = contextlib.ExitStack()
            MQS = p1.enter_context(nc.sbuf_tensor("MQS", [P, ND * 512], BF16))
            MKS = p1.enter_context(nc.sbuf_tensor("MKS", [P, ND * 512], BF16))
            MVS = p1.enter_context(nc.sbuf_tensor("MVS", [P, ND * D], BF16))
            DUMP = p1.enter_context(nc.sbuf_tensor("DUMP", [P, D], BF16))
            XNS0 = p1.enter_context(nc.sbuf_tensor("XNS0", [P, D], BF16))
            XNS1 = p1.enter_context(nc.sbuf_tensor("XNS1", [P, D], BF16))
            LOG = p1.enter_context(nc.sbuf_tensor("LOG", [P, NT * E], F32))
            NMXb = p1.enter_context(nc.sbuf_tensor("NMXb", [P, NT], F32))
            PB = p1.enter_context(nc.sbuf_tensor("PB", [P, NT * E], F32))
            SZb = p1.enter_context(nc.sbuf_tensor("SZb", [P, NT], F32))
            RZb = p1.enter_context(nc.sbuf_tensor("RZb", [P, NT], F32))
            PRb = p1.enter_context(nc.sbuf_tensor("PRb", [P, NT * E], F32))
            M1b = p1.enter_context(nc.sbuf_tensor("M1b", [P, NT], F32))
            M2b = p1.enter_context(nc.sbuf_tensor("M2b", [P, NT], F32))
            MKb = p1.enter_context(nc.sbuf_tensor("MKb", [P, NT * E], F32))
            MDb = p1.enter_context(nc.sbuf_tensor("MDb", [P, NT * E], F32))
            EM1b = p1.enter_context(nc.sbuf_tensor("EM1b", [P, NT], F32))
            EM2b = p1.enter_context(nc.sbuf_tensor("EM2b", [P, NT], F32))
            EPb = p1.enter_context(nc.sbuf_tensor("EPb", [P, NT * E], F32))
            DDb = p1.enter_context(nc.sbuf_tensor("DDb", [P, NT], F32))
            RDb = p1.enter_context(nc.sbuf_tensor("RDb", [P, NT], F32))
            CAb = p1.enter_context(nc.sbuf_tensor("CAb", [P, NT * E], F32))
            CBb = p1.enter_context(nc.sbuf_tensor("CBb", [P, NT * E], F32))
            PJ0 = p1.enter_context(nc.psum_tensor("PJ0", [P, 512], F32))
            PJ1 = p1.enter_context(nc.psum_tensor("PJ1", [P, 512], F32))
            with p1:
                XNS = [XNS0, XNS1]
                PJ = [PJ0, PJ1]

                @blk.gpsimd
                def _(g):
                    g.dma_start(MQS[:, :].rearrange("p (dt j) -> p dt j", dt=ND), mq_d.rearrange("(dt p) j -> p dt j", p=P)).then_inc(sem["s_wg"], 16)
                    g.dma_start(MKS[:, :].rearrange("p (dt j) -> p dt j", dt=ND), mk_d.rearrange("(dt p) j -> p dt j", p=P)).then_inc(sem["s_wg"], 16)
                    g.dma_start(MVS[:, :].rearrange("p (dt j) -> p dt j", dt=ND), mv_d.rearrange("(dt p) j -> p dt j", p=P)).then_inc(sem["s_wg"], 16)
                    g.wait_ge(sem["s_wg"], 48)
                    g.dma_start(RWTS[:, :].rearrange("p (dt j) -> p dt j", dt=ND), rwt_d.rearrange("(dt p) j -> p dt j", p=P)).then_inc(sem["s_wg"], 16)

                @blk.sync
                def _(s):
                    s.dma_start(ESC[:, :], esc_d).then_inc(sem["s_hw"], 16)
                    for tt in range(NT):
                        if tt >= 4:
                            s.wait_ge(sem["s_x"], max(64, 16 * (tt - 3)))
                        s.dma_start(X[:, tt * D:(tt + 1) * D],
                                    x_d[tt * P:(tt + 1) * P, :]).then_inc(sem["s_x"], 16)
                    for tt in range(NT):
                        s.wait_ge(sem["s_xns"], tt + 1)
                        for dt in range(ND):
                            s.dma_start(XNT[:, dt * T + tt * P: dt * T + (tt + 1) * P],
                                        XNS[tt % 2][:, dt * P:(dt + 1) * P],
                                        transpose=True).then_inc(sem["s_xnt"], 16)

                @blk.tensor
                def _(t):
                    t.wait_ge(sem["s_wg"], WG_MV)
                    gidx = 0
                    for tt in range(NT):
                        t.wait_ge(sem["s_xnt"], 16 * ND * (tt + 1))
                        for jc in range(2):
                            ps = PJ[gidx % 2]
                            if gidx >= 2:
                                t.wait_ge(sem["s_pjev"], gidx - 1)
                            for dt in range(ND):
                                ins = t.matmul(ps[:, 0:512],
                                               XNT[:, dt * T + tt * P: dt * T + (tt + 1) * P],
                                               MVS[:, dt * D + jc * 512: dt * D + (jc + 1) * 512],
                                               start=(dt == 0), stop=(dt == ND - 1))
                            ins.then_inc(sem["s_pj"], 1)
                            gidx += 1
                    t.wait_ge(sem["s_xnt"], 16 * ND * NT)
                    for wsb in (MQS, MKS):
                        for rp in range(4):
                            for nb in range(4):
                                ps = PJ[gidx % 2]
                                t.wait_ge(sem["s_pjev"], gidx - 1)
                                for dt in range(ND):
                                    ins = t.matmul(ps[:, 0:512],
                                                   wsb[:, dt * 512 + rp * P: dt * 512 + (rp + 1) * P],
                                                   XNT[:, dt * T + nb * 512: dt * T + (nb + 1) * 512],
                                                   start=(dt == 0), stop=(dt == ND - 1))
                                ins.then_inc(sem["s_pj"], 1)
                                gidx += 1
                    t.wait_ge(sem["s_wg"], WG_RWT)
                    t.wait_ge(sem["s_pjev"], N_PJEV)
                    for tt in range(NT):
                        ps = PJ[tt % 2]
                        if tt >= 2:
                            t.wait_ge(sem["s_rt"], tt - 1)
                        for dt in range(ND):
                            ins = t.matmul(ps[:, 0:E],
                                           XNT[:, dt * T + tt * P: dt * T + (tt + 1) * P],
                                           RWTS[:, dt * E:(dt + 1) * E],
                                           start=(dt == 0), stop=(dt == ND - 1))
                        ins.then_inc(sem["s_pj"], 1)

                @blk.scalar
                def _(sc):
                    for tt in range(NT):
                        sc.wait_ge(sem["s_x"], 64 if tt < 4 else 16 * (tt + 1))
                        sc.activation(DUMP[:, :], X[:, tt * D:(tt + 1) * D], AF.Square,
                                      accum_out=SS[:, tt:tt + 1]).then_inc(sem["s_ln2"], 1)
                    sc.wait_ge(sem["s_ln3"], 1)
                    sc.activation(SD[:, 0:NT], VEPS[:, 0:NT], AF.Sqrt).then_inc(sem["s_ln4"], 1)
                    gidx = 0
                    for tt in range(NT):
                        for jc in range(2):
                            sc.wait_ge(sem["s_pj"], gidx + 1)
                            sc.copy(VP[:, tt * D + jc * 512: tt * D + (jc + 1) * 512],
                                    PJ[gidx % 2][:, 0:512]).then_inc(sem["s_pjev"], 1)
                            gidx += 1
                    for dst in (QT, KT):
                        for rp in range(4):
                            for nb in range(4):
                                sc.wait_ge(sem["s_pj"], gidx + 1)
                                sc.copy(dst[:, rp * T + nb * 512: rp * T + (nb + 1) * 512],
                                        PJ[gidx % 2][:, 0:512]).then_inc(sem["s_pjev"], 1)
                                gidx += 1
                    sc.wait_ge(sem["s_rv"], 1)
                    for tt in range(NT):
                        ins = sc.activation(PB[:, tt * E:(tt + 1) * E], LOG[:, tt * E:(tt + 1) * E],
                                            AF.Exp, bias=NMXb[:, tt:tt + 1],
                                            accum_out=SZb[:, tt:tt + 1])
                    ins.then_inc(sem["s_rs"], 1)
                    sc.wait_ge(sem["s_rv"], 2)
                    sc.activation(EM1b[:, 0:NT], M1b[:, 0:NT], AF.Exp)
                    sc.activation(EM2b[:, 0:NT], M2b[:, 0:NT], AF.Exp)
                    sc.activation(EPb[:, :], PRb[:, :], AF.Exp).then_inc(sem["s_rs"], 1)

                @blk.vector
                def _(v):
                    v.memset(ZZ[:, :], 0.0).then_inc(sem["s_zz"], 1)
                    for tt in range(NT):
                        v.wait_ge(sem["s_x"], 64 if tt < 4 else 16 * (tt + 1))
                        v.reduce_sum(S1[:, tt:tt + 1], X[:, tt * D:(tt + 1) * D], AX.X)
                    for tt in range(NT):
                        v.tensor_scalar(NM[:, tt:tt + 1], S1[:, tt:tt + 1], -1.0 / D, None,
                                        ALU.mult)
                    v.wait_ge(sem["s_ln2"], NT)
                    v.tensor_scalar(V2[:, 0:NT], SS[:, 0:NT], 1.0 / D, 1e-5,
                                    ALU.mult, ALU.add)
                    v.tensor_tensor(MSQ[:, 0:NT], NM[:, 0:NT], NM[:, 0:NT], ALU.mult)
                    v.drain()
                    v.tensor_tensor(VEPS[:, 0:NT], V2[:, 0:NT], MSQ[:, 0:NT],
                                    ALU.subtract).then_inc(sem["s_ln3"], 1)
                    v.wait_ge(sem["s_ln4"], 1)
                    v.reciprocal(RSTD[:, 0:NT], SD[:, 0:NT])
                    v.drain()
                    v.tensor_tensor(NMRS[:, 0:NT], NM[:, 0:NT], RSTD[:, 0:NT], ALU.mult)
                    v.drain()
                    for tt in range(NT):
                        if tt >= 2:
                            v.wait_ge(sem["s_xnt"], 16 * ND * (tt - 1))
                        v.tensor_scalar(XNS[tt % 2][:, :], X[:, tt * D:(tt + 1) * D],
                                        RSTD[:, tt:tt + 1], NMRS[:, tt:tt + 1],
                                        ALU.mult, ALU.add).then_inc(sem["s_xns"], 1)
                    v.wait_ge(sem["s_hw"], 16)
                    for tt in range(NT):
                        v.wait_ge(sem["s_pj"], N_PJEV + tt + 1)
                        v.tensor_copy(LOG[:, tt * E:(tt + 1) * E],
                                      PJ[tt % 2][:, 0:E]).then_inc(sem["s_rt"], 1)
                    for tt in range(NT):
                        ins = v.tensor_reduce(NMXb[:, tt:tt + 1], LOG[:, tt * E:(tt + 1) * E],
                                              AX.X, ALU.max, negate=True)
                    ins.then_inc(sem["s_rv"], 1)
                    v.wait_ge(sem["s_rs"], 1)
                    for tt in range(NT):
                        v.reciprocal(RZb[:, tt:tt + 1], SZb[:, tt:tt + 1])
                    for tt in range(NT):
                        v.tensor_scalar(PRb[:, tt * E:(tt + 1) * E], PB[:, tt * E:(tt + 1) * E],
                                        RZb[:, tt:tt + 1], None, ALU.mult)
                    for tt in range(NT):
                        v.tensor_reduce(M1b[:, tt:tt + 1], PRb[:, tt * E:(tt + 1) * E], AX.X, ALU.max)
                    for tt in range(NT):
                        v.tensor_scalar(MKb[:, tt * E:(tt + 1) * E], PRb[:, tt * E:(tt + 1) * E],
                                        M1b[:, tt:tt + 1], None, ALU.is_ge)
                    for tt in range(NT):
                        v.scalar_tensor_tensor(MDb[:, tt * E:(tt + 1) * E], MKb[:, tt * E:(tt + 1) * E],
                                               -10.0, PRb[:, tt * E:(tt + 1) * E], ALU.mult, ALU.add)
                    for tt in range(NT):
                        ins = v.tensor_reduce(M2b[:, tt:tt + 1], MDb[:, tt * E:(tt + 1) * E],
                                              AX.X, ALU.max)
                    ins.then_inc(sem["s_rv"], 1)
                    v.wait_ge(sem["s_rs"], 2)
                    v.tensor_tensor(DDb[:, 0:NT], EM1b[:, 0:NT], EM2b[:, 0:NT], ALU.add)
                    for tt in range(NT):
                        v.tensor_scalar(MKb[:, tt * E:(tt + 1) * E], PRb[:, tt * E:(tt + 1) * E],
                                        M2b[:, tt:tt + 1], None, ALU.is_ge)
                    v.reciprocal(RDb[:, 0:NT], DDb[:, 0:NT])
                    v.tensor_tensor(CAb[:, :], EPb[:, :], MKb[:, :], ALU.mult)
                    v.drain()
                    for tt in range(NT):
                        v.tensor_scalar(CBb[:, tt * E:(tt + 1) * E], CAb[:, tt * E:(tt + 1) * E],
                                        RDb[:, tt:tt + 1], None, ALU.mult)
                    for tt in range(NT):
                        v.tensor_tensor(COMB[:, tt * E:(tt + 1) * E], CBb[:, tt * E:(tt + 1) * E],
                                        ESC[:, :], ALU.mult)

            PSA = ctx.enter_context(nc.psum_tensor("PSA", [P, 1024], F32))
            PSB = ctx.enter_context(nc.psum_tensor("PSB", [P, 1024], F32))
            AVP0 = ctx.enter_context(nc.psum_tensor("AVP0", [P, 1024], F32))
            AVP1 = ctx.enter_context(nc.psum_tensor("AVP1", [P, 1024], F32))
            with (nc.sbuf_tensor("QHp", [P, T], BF16) as QHp,
                  nc.sbuf_tensor("KHp", [P, T], BF16) as KHp,
                  nc.sbuf_tensor("EB0", [P, T], BF16) as EB0,
                  nc.sbuf_tensor("EB1", [P, T], BF16) as EB1,
                  nc.sbuf_tensor("E20", [P, T], BF16) as E20,
                  nc.sbuf_tensor("E21", [P, T], BF16) as E21):
                EB = [EB0, EB1]
                E2 = [E20, E21]
                AVP = [AVP0, AVP1]

                @blk.sync
                def _(s):
                    s.wait_ge(sem["s_pjev"], N_PJEV)
                    for h in range(H):
                        hb = (h % 2) * 64
                        if h >= 1:
                            s.wait_ge(sem["s_gather"], 256 * h)
                        if h >= 2:
                            s.wait_ge(sem["s_exp"], 32 * (h - 1))
                        for (buf, src) in ((QHp, QT), (KHp, KT)):
                            for pi in range(2):
                                for rp in range(4):
                                    s.dma_start(
                                        buf[hb:hb + 64, (2 * rp + pi) * 256:(2 * rp + pi + 1) * 256],
                                        src[pi * 64: pi * 64 + 64,
                                            rp * T + h * 256: rp * T + (h + 1) * 256],
                                    ).then_inc(sem["s_gather"], 16)

                @blk.tensor
                def _(t):
                    t.wait_ge(sem["s_rt"], NT)
                    t.wait_ge(sem["s_zz"], 1)
                    for h in range(H):
                        hb = (h % 2) * 64
                        t.wait_ge(sem["s_gather"], 256 * (h + 1))
                        if h >= 1:
                            t.wait_ge(sem["s_avev"], 2 * h)
                        for ha in range(2):
                            for bk in range(2):
                                t.matmul(AVP[ha][:, bk * 512:(bk + 1) * 512],
                                         ZZ[:, 0:128], ZZ[:, 0:512],
                                         start=True, stop=False)
                        def emit_av(j):
                            t.wait_ge(sem["s_scl"], 16 * h + j + 1)
                            rc, halfc = j // 2, j % 2
                            nmm = 0
                            for ha in range(2):
                                for r in range(8):
                                    nmm += 1
                                    ins = t.matmul(
                                        AVP[ha][:, r * P:(r + 1) * P],
                                        E2[j % 2][:, r * 256 + ha * P: r * 256 + ha * P + P],
                                        VP[:, (2 * r + halfc) * D + rc * P: (2 * r + halfc) * D + (rc + 1) * P],
                                        start=False, stop=(j == 15) and (r % 4 == 3))
                                    if nmm == 16:
                                        ins.then_inc(sem["s_av"], 1)
                        for i in range(16):
                            for half, ps in ((0, PSA), (1, PSB)):
                                w = 32 * h + 2 * (i - 1) + half + 1
                                if w > 0:
                                    t.wait_ge(sem["s_exp"], w)
                                for u in range(2):
                                    ins = t.matmul(ps[:, u * 512:(u + 1) * 512],
                                                   KHp[hb:hb + 64, i * P:(i + 1) * P],
                                                   QHp[hb:hb + 64,
                                                       half * 1024 + u * 512: half * 1024 + (u + 1) * 512],
                                                   start=True, stop=True)
                                ins.then_inc(sem["s_mm"], 1)
                            if i >= 1:
                                emit_av(i - 1)
                        emit_av(15)

                @blk.scalar
                def _(sc):
                    for h in range(H):
                        for i in range(16):
                            sc.wait_ge(sem["s_vmax"], 16 * h + i + 1)
                            w = 16 * h + i
                            if w > 0:
                                sc.wait_ge(sem["s_scl"], w)
                            w2 = 16 * h + i - 1
                            if w2 > 0:
                                sc.wait_ge(sem["s_av"], w2)
                            sc.activation(EB[i % 2][:, 0:1024], PSA[:, 0:1024], AF.Exp,
                                          bias=NMmin8[:, 0:1], scale=0.125,
                                          accum_out=Z2p[i % 2][:, 0:1]).then_inc(sem["s_exp"], 1)
                            sc.activation(EB[i % 2][:, 1024:2048], PSB[:, 0:1024], AF.Exp,
                                          bias=NMmin8[:, 0:1], scale=0.125,
                                          accum_out=Z2p[i % 2][:, 1:2]).then_inc(sem["s_exp"], 1)

                @blk.vector
                def _(v):
                    for h in range(H):
                        for i in range(17):
                            j = i - 1
                            if j >= 0:
                                v.wait_ge(sem["s_exp"], 32 * h + 2 * j + 2)
                                v.reduce_sum(Zs[:, 0:1], Z2p[j % 2][:, 0:2], AX.X)
                            if i < 16:
                                base = 32 * h + 2 * i
                                if base > 0:
                                    v.wait_ge(sem["s_exp"], base)
                                v.wait_ge(sem["s_mm"], base + 1)
                                v.tensor_reduce(NMmin[:, 0:1], PSA[:, 0:512], AX.X,
                                                ALU.max, negate=True)
                            if j >= 0:
                                if i >= 16:
                                    v.drain()
                                v.reciprocal(Rz[:, 0:1], Zs[:, 0:1])
                            if i < 16:
                                if j < 0:
                                    v.drain()
                                v.tensor_scalar(NMmin8[:, 0:1], NMmin[:, 0:1], 0.125, None,
                                                ALU.mult).then_inc(sem["s_vmax"], 1)
                            if j >= 0:
                                w = 16 * h + j - 1
                                if w > 0:
                                    v.wait_ge(sem["s_av"], w)
                                if i >= 16:
                                    v.drain()
                                v.tensor_scalar(E2[j % 2][:, 0:1024], EB[j % 2][:, 0:1024],
                                                Rz[:, 0:1], None, ALU.mult)
                                v.tensor_scalar(E2[j % 2][:, 1024:2048], EB[j % 2][:, 1024:2048],
                                                Rz[:, 0:1], None, ALU.mult).then_inc(sem["s_scl"], 1)
                        v.wait_ge(sem["s_av"], 16 * (h + 1))
                        for ha in range(2):
                            tt = 2 * h + ha
                            v.tensor_tensor(X[:, tt * D:(tt + 1) * D], X[:, tt * D:(tt + 1) * D],
                                            AVP[ha][:, 0:1024], ALU.add).then_inc(sem["s_avev"], 1)

                if DEBUG_STAGE == "attn":
                    @blk.sync
                    def _(s):
                        s.wait_ge(sem["s_avev"], 2 * H)
                        s.dma_start(out_d.rearrange("(tt p) d -> p tt d", p=P),
                                    X[:, :].rearrange("p (tt d) -> p tt d", d=D)).then_inc(sem["s_out"], 16)
                        s.dma_start(dbg["d_xnt"], XNT[:, :]).then_inc(sem["s_out"], 16)
                        s.dma_start(dbg["d_vp"], VP[:, :]).then_inc(sem["s_out"], 16)
                        s.dma_start(dbg["d_qt"], QT[:, :]).then_inc(sem["s_out"], 16)
                        s.dma_start(dbg["d_kt"], KT[:, :]).then_inc(sem["s_out"], 16)
                        s.dma_start(dbg["d_comb"], COMB[:, :]).then_inc(sem["s_out"], 16)
                        for nm_, tt_ in [("d_s1", S1), ("d_nm", NM), ("d_ss", SS), ("d_veps", VEPS),
                                         ("d_sd", SD), ("d_rstd", RSTD), ("d_nmrs", NMRS)]:
                            s.dma_start(dbg[nm_], tt_[:, :]).then_inc(sem["s_out"], 16)
                        s.wait_ge(sem["s_out"], 208)



        if DEBUG_STAGE == "attn":
            return nc
        with (nc.sbuf_tensor("W1Q0", [P, ND * 1024], BF16) as W1Q0,
              nc.sbuf_tensor("W2Q0", [P, FT_Q * 1024], BF16) as W2Q0,
              nc.sbuf_tensor("W1Q1", [P, ND * 1024], BF16) as W1Q1,
              nc.sbuf_tensor("W2Q1", [P, FT_Q * 1024], BF16) as W2Q1,
              nc.sbuf_tensor("SH0", [P, FT_Q * 256], BF16) as SH0,
              nc.sbuf_tensor("SH1", [P, FT_Q * 256], BF16) as SH1):
            W1Q = [W1Q0, W1Q1]
            W2Q = [W2Q0, W2Q1]
            SH = [SH0, SH1]
            YPS = [PSA, PSB]
            HPS = AVP

            @blk.gpsimd
            def _(g):
                for idx in range(NIDX):
                    e, q = divmod(idx, NQ)
                    par = idx % 2
                    if idx == 0:
                        g.wait_ge(sem["s_gather"], 256 * H)
                    elif idx == 1:
                        g.wait_ge(sem["s_avev"], 2 * H)
                    else:
                        g.wait_ge(sem["s_yev"], 2 * TC * (idx - 1))
                    src1 = w1t_d[e].rearrange("(dt p) f -> p dt f", p=P)[:, :, q * 1024:(q + 1) * 1024]
                    dst1 = W1Q[par][:, :].rearrange("p (dt f) -> p dt f", dt=ND)
                    g.dma_start(dst1, src1).then_inc(sem["s_wg"], 16)
                    src2 = w2t_d[e, q * 1024:(q + 1) * 1024, :].rearrange("(ft p) d -> p ft d", p=P)
                    dst2 = W2Q[par][:, :].rearrange("p (ft d) -> p ft d", ft=FT_Q)
                    g.dma_start(dst2, src2).then_inc(sem["s_wg"], 16)

            @blk.tensor
            def _(t):
                t.wait_ge(sem["s_avev"], 2 * H)
                for idx in range(NIDX):
                    par = idx % 2
                    t.wait_ge(sem["s_wg"], wg_q(idx))
                    for tc in range(TC):
                        gtc = idx * TC + tc
                        for ft in range(FT_Q):
                            g = gtc * FT_Q + ft
                            if g >= 2:
                                t.wait_ge(sem["s_silu"], g - 1)
                            hp = HPS[g % 2]
                            for dt in range(ND):
                                ins = t.matmul(hp[:, 0:256],
                                               W1Q[par][:, dt * 1024 + ft * P: dt * 1024 + (ft + 1) * P],
                                               XNT[:, dt * T + tc * 256: dt * T + (tc + 1) * 256],
                                               start=(dt == 0), stop=(dt == ND - 1))
                            ins.then_inc(sem["s_m1"], 1)
                        if gtc >= 1:
                            t.wait_ge(sem["s_yev"], 2 * gtc)
                        for ft in range(FT_Q):
                            g = gtc * FT_Q + ft
                            t.wait_ge(sem["s_silu"], g + 1)
                            for tb in range(2):
                                for nh in range(2):
                                    ins = t.matmul(
                                        YPS[tb][:, nh * 512:(nh + 1) * 512],
                                        SH[tc % 2][:, ft * 256 + tb * P: ft * 256 + (tb + 1) * P],
                                        W2Q[par][:, ft * 1024 + nh * 512: ft * 1024 + (nh + 1) * 512],
                                        start=(ft == 0), stop=(ft == FT_Q - 1))
                                    if ft == FT_Q - 1 and tb == 1 and nh == 1:
                                        ins.then_inc(sem["s_m2"], 1)

            @blk.scalar
            def _(sc):
                for idx in range(NIDX):
                    for tc in range(TC):
                        gtc = idx * TC + tc
                        for ft in range(FT_Q):
                            g = gtc * FT_Q + ft
                            sc.wait_ge(sem["s_m1"], g + 1)
                            if gtc >= 2 and ft == 0:
                                sc.wait_ge(sem["s_m2"], gtc - 1)
                            sc.activation(SH[tc % 2][:, ft * 256:(ft + 1) * 256],
                                          HPS[g % 2][:, 0:256],
                                          AF.Silu).then_inc(sem["s_silu"], 1)

            @blk.vector
            def _(v):
                for idx in range(NIDX):
                    e, q = divmod(idx, NQ)
                    for tc in range(TC):
                        gtc = idx * TC + tc
                        v.wait_ge(sem["s_m2"], gtc + 1)
                        for tb in range(2):
                            tt = tc * 2 + tb
                            v.scalar_tensor_tensor(
                                X[:, tt * D:(tt + 1) * D],
                                YPS[tb][:, 0:1024],
                                COMB[:, tt * E + e: tt * E + e + 1],
                                X[:, tt * D:(tt + 1) * D],
                                ALU.mult, ALU.add).then_inc(sem["s_yev"], 1)

            @blk.sync
            def _(s):
                for tc in range(TC):
                    s.wait_ge(sem["s_yev"], 2 * ((NIDX - 1) * TC + tc) + 2)
                    for tb in range(2):
                        tt = tc * 2 + tb
                        s.dma_start(out_d[tt * P:(tt + 1) * P, :],
                                    X[:, tt * D:(tt + 1) * D]).then_inc(sem["s_out"], 16)
                s.wait_ge(sem["s_out"], 16 * NT)


    return nc


_NC_CACHE = {}


def kernel(L0_attn_x, L0_attn_Mq, L0_attn_Mk, L0_attn_Mv,
           L0_ffn_router_w, L0_ffn_w1, L0_ffn_w2, L0_ffn_expert_scale):
    x = np.ascontiguousarray(np.asarray(L0_attn_x, dtype=np.float32))
    mq = np.ascontiguousarray(np.asarray(L0_attn_Mq, dtype=np.float32))
    mk = np.ascontiguousarray(np.asarray(L0_attn_Mk, dtype=np.float32))
    mv = np.ascontiguousarray(np.asarray(L0_attn_Mv, dtype=np.float32))
    rwt = np.ascontiguousarray(np.asarray(L0_ffn_router_w, dtype=np.float32).T)
    w1t = np.ascontiguousarray(np.asarray(L0_ffn_w1, dtype=np.float32).transpose(0, 2, 1))
    w2t = np.ascontiguousarray(np.asarray(L0_ffn_w2, dtype=np.float32).transpose(0, 2, 1))
    esc = np.ascontiguousarray(
        np.broadcast_to(np.asarray(L0_ffn_expert_scale, dtype=np.float32)[None, :], (P, E)))

    if "nc" not in _NC_CACHE:
        _NC_CACHE["nc"] = build_nc()
    nc = _NC_CACHE["nc"]

    in_maps = [{"x": np.ascontiguousarray(x[b]), "mq": mq, "mk": mk, "mv": mv,
                "rwt": rwt, "esc": esc, "w1t": w1t, "w2t": w2t} for b in range(8)]
    res = run_bass_kernel_spmd(nc, in_maps, core_ids=list(range(8)))
    return np.stack([np.asarray(res.results[b]["out"], dtype=np.float32) for b in range(8)], axis=0)
